# version 01 (final)
# baseline (speedup 1.0000x reference)
import numpy as np
import concourse.bass as bass
import concourse.mybir as mybir
from concourse.bass_utils import run_bass_kernel_spmd
from contextlib import ExitStack

F = mybir.dt.float32
AF = mybir.ActivationFunctionType
ALU = mybir.AluOpType
AX = mybir.AxisListType
SCALE = 0.08838834764831845
NCTRL = 130156
NS = 16777216

_CACHE = {}


def build_nc():
    nc = bass.Bass("TRN2", target_bir_lowering=False)
    I = lambda name, shp: nc.dram_tensor(name, shp, F, kind="ExternalInput").ap()
    xf = I("xf", [4096, 2048])
    xo = I("xo", [2048, 2048])
    wqT = I("wqT", [2048, 2048])
    wkT = I("wkT", [2048, 512])
    wvT = I("wvT", [2048, 512])
    woT = I("woT", [2048, 2048])
    cosf = I("cosf", [128, 4096])
    sinf = I("sinf", [128, 4096])
    coso = I("coso", [128, 2048])
    sino = I("sino", [128, 2048])
    maskT = I("maskT", [4096, 2048])
    identp = I("identp", [128, 128])
    pg = [I(f"pg{j}", [2048, 8192]) for j in range(4)]
    pu = [I(f"pu{j}", [2048, 8192]) for j in range(4)]
    cg = [I(f"cg{j}", [2048, 8192]) for j in range(4)]
    pd = [I(f"pd{j}", [8192, 2048]) for j in range(4)]
    cd = [I(f"cd{j}", [8192, 2048]) for j in range(4)]
    y = nc.dram_tensor("y", [2048, 2048], F, kind="ExternalOutput").ap()
    S = lambda name, shp: nc.dram_tensor(name, shp, F, kind="Internal").ap()
    q_scr = S("q_scr", [16, 128, 2048])
    v_scr = S("v_scr", [4, 4096, 128])
    ao_scr = S("ao_scr", [16, 128, 2048])
    x1_scr = S("x1_scr", [2048, 2048])
    l2_scr = S("l2_scr", [16, 128, 2048])
    ia_scr = S("ia_scr", [8192, 2048])
    gw_scr = S("gw_scr", [64, 128, 2048])
    uw_scr = S("uw_scr", [64, 128, 2048])

    with ExitStack() as stk:
        ec = stk.enter_context
        A = ec(nc.sbuf_tensor("A", [128, 16384], F))
        B = ec(nc.sbuf_tensor("B", [128, 16384], F))
        C = ec(nc.sbuf_tensor("C", [128, 4096], F))
        D = ec(nc.sbuf_tensor("D", [128, 4096], F))
        E = ec(nc.sbuf_tensor("E", [128, 4096], F))
        CS1 = ec(nc.sbuf_tensor("CS1", [128, 512], F))
        CS2 = ec(nc.sbuf_tensor("CS2", [128, 512], F))
        ST = ec(nc.sbuf_tensor("ST", [128, 16], F))
        ONE = ec(nc.sbuf_tensor("ONE", [128, 128], F))
        IDT = ec(nc.sbuf_tensor("IDT", [128, 128], F))
        ps = [ec(nc.psum_tensor(f"ps{i}", [128, 512], F)) for i in range(8)]
        sem = ec(nc.semaphore("G"))

        sched = []
        cnt = [0]

        def op(eng, inc, fn):
            sched.append((eng, cnt[0], fn, inc))
            cnt[0] += inc

        def dma(out, in_):
            op("sync", 16, lambda e, o=out, i=in_: e.dma_start(out=o, in_=i))

        def mm(out, l, r, s, t):
            op("tensor", 1, lambda e, o=out, ll=l, rr=r, s_=s, t_=t:
               e.matmul(o, lhsT=ll, rhs=rr, start=s_, stop=t_, skip_group_check=True))

        def tr(out, in_):
            op("tensor", 1, lambda e, o=out, i=in_: e.transpose(o, i, IDT[:, :]))

        def cps(out, in_):
            op("scalar", 1, lambda e, o=out, i=in_: e.copy(o, i))

        def act(out, in_, func, bias=None, scale=None):
            kw = {}
            if bias is not None:
                kw["bias"] = bias
            if scale is not None:
                kw["scale"] = scale
            op("scalar", 1, lambda e, o=out, i=in_, f=func, k=kw: e.activation(o, i, f, **k))

        def smul(out, in_, c):
            op("scalar", 1, lambda e, o=out, i=in_, c_=c: e.mul(o, i, c_))

        def vadd(out, a, b):
            op("vector", 1, lambda e, o=out, a_=a, b_=b: e.tensor_add(o, a_, b_))

        def vmul(out, a, b):
            op("vector", 1, lambda e, o=out, a_=a, b_=b: e.tensor_mul(o, a_, b_))

        def vcp(out, in_):
            op("vector", 1, lambda e, o=out, i=in_: e.tensor_copy(o, i))

        def tsadd(out, in_, s):
            op("vector", 1, lambda e, o=out, i=in_, s_=s: e.tensor_scalar_add(o, i, s_))

        def tsmul(out, in_, s):
            op("vector", 1, lambda e, o=out, i=in_, s_=s: e.tensor_scalar_mul(o, i, s_))

        def red(out, in_, o_):
            op("vector", 1, lambda e, ou=out, i=in_, oo=o_: e.tensor_reduce(ou, i, AX.X, oo))

        def recip(out, in_):
            op("vector", 1, lambda e, o=out, i=in_: e.reciprocal(o, i))

        def memset(ap, v):
            op("vector", 1, lambda e, a=ap, v_=v: e.memset(a, v_))

        dma(IDT[:, :], identp[:, :])
        memset(ONE[:, :], 1.0)
        _consts = {5: 1e-5, 6: 1.0 / 2048.0, 7: SCALE, 8: -1.0 / 2048.0, 9: 1.0}
        for col, val in _consts.items():
            memset(ST[:, col:col + 1], val)
            nc.const_aps.register(F, val, ST[:, col:col + 1])

        def layernorm(xap, xcap, sqap, lnap):
            red(ST[:, 0:1], xap, ALU.add)
            smul(ST[:, 1:2], ST[:, 0:1], -1.0 / 2048.0)
            tsadd(xcap, xap, ST[:, 1:2])
            act(sqap, xcap, AF.Square)
            red(ST[:, 2:3], sqap, ALU.add)
            act(ST[:, 3:4], ST[:, 2:3], AF.Sqrt, bias=1e-5, scale=1.0 / 2048.0)
            recip(ST[:, 4:5], ST[:, 3:4])
            tsmul(lnap, xcap, ST[:, 4:5])

        def ln_chunk_to_B(src_dram, row0):
            for sub in range(4):
                dma(D[:, 0:2048], src_dram[row0 + sub * 128: row0 + (sub + 1) * 128, :])
                layernorm(D[:, 0:2048], E[:, 0:2048], D[:, 2048:4096], D[:, 0:2048])
                for hc in range(16):
                    tr(ps[7][:, 0:128], D[:, hc * 128:(hc + 1) * 128])
                    cps(B[:, hc * 512 + sub * 128: hc * 512 + sub * 128 + 128], ps[7][:, 0:128])

        def rope(psrc, cosap, sinap, dstap):
            cps(D[:, 0:512], psrc)
            vcp(D[0:64, 512:1024], D[64:128, 0:512])
            vcp(D[64:128, 512:1024], D[0:64, 0:512])
            vmul(D[:, 1024:1536], D[:, 0:512], cosap)
            vmul(D[:, 1536:2048], D[:, 512:1024], sinap)
            vadd(dstap, D[:, 1024:1536], D[:, 1536:2048])

        for tc in range(8):
            ln_chunk_to_B(xf, tc * 512)
            dma(CS1[:, :], cosf[:, tc * 512:(tc + 1) * 512])
            dma(CS2[:, :], sinf[:, tc * 512:(tc + 1) * 512])
            for hc in range(16):
                dma(C[:, 0:512], wkT[hc * 128:(hc + 1) * 128, :])
                dma(C[:, 512:1024], wvT[hc * 128:(hc + 1) * 128, :])
                for d4 in range(4):
                    mm(ps[d4][:, :], C[:, d4 * 128:(d4 + 1) * 128],
                       B[:, hc * 512:(hc + 1) * 512], hc == 0, hc == 15)
                    mm(ps[4 + d4][:, :], C[:, 512 + d4 * 128:512 + (d4 + 1) * 128],
                       B[:, hc * 512:(hc + 1) * 512], hc == 0, hc == 15)
            for d4 in range(4):
                rope(ps[d4][:, :], CS1[:, :], CS2[:, :],
                     A[:, d4 * 4096 + tc * 512: d4 * 4096 + (tc + 1) * 512])
            for d4 in range(4):
                cps(E[:, 0:512], ps[4 + d4][:, :])
                for j in range(4):
                    tr(ps[0][:, 0:128], E[:, j * 128:(j + 1) * 128])
                    cps(E[:, 512:640], ps[0][:, 0:128])
                    dma(v_scr[d4, tc * 512 + j * 128: tc * 512 + (j + 1) * 128, :],
                        E[:, 512:640])

        for tc in range(4):
            ln_chunk_to_B(xo, tc * 512)
            dma(CS1[:, :], coso[:, tc * 512:(tc + 1) * 512])
            dma(CS2[:, :], sino[:, tc * 512:(tc + 1) * 512])
            for g2 in range(2):
                for hc in range(16):
                    dma(C[:, 0:1024], wqT[hc * 128:(hc + 1) * 128,
                                          g2 * 1024:(g2 + 1) * 1024])
                    for h8 in range(8):
                        mm(ps[h8][:, :], C[:, h8 * 128:(h8 + 1) * 128],
                           B[:, hc * 512:(hc + 1) * 512], hc == 0, hc == 15)
                for h8 in range(8):
                    hd = g2 * 8 + h8
                    rope(ps[h8][:, :], CS1[:, :], CS2[:, :], E[:, 0:512])
                    dma(q_scr[hd, :, tc * 512:(tc + 1) * 512], E[:, 0:512])

        for qg in range(4):
            for hd in range(16):
                kvh = hd // 4
                dma(E[:, 512:1024], q_scr[hd, :, qg * 512:(qg + 1) * 512])
                for kb in range(32):
                    mm(ps[0][:, :], A[:, kvh * 4096 + kb * 128: kvh * 4096 + (kb + 1) * 128],
                       E[:, 512:1024], True, True)
                    dma(E[:, 1024:1536], maskT[kb * 128:(kb + 1) * 128,
                                               qg * 512:(qg + 1) * 512])
                    vadd(ps[0][:, :], ps[0][:, :], E[:, 1024:1536])
                    act(B[:, kb * 512:(kb + 1) * 512], ps[0][:, :], AF.Exp, scale=SCALE)
                    dma(E[:, 0:128], v_scr[kvh, kb * 128:(kb + 1) * 128, :])
                    mm(ps[1][:, :], E[:, 0:128], B[:, kb * 512:(kb + 1) * 512],
                       kb == 0, kb == 31)
                    mm(ps[2][0:1, :], ONE[:, 0:1], B[:, kb * 512:(kb + 1) * 512],
                       kb == 0, kb == 31)
                recip(CS1[0:1, 0:512], ps[2][0:1, 0:512])
                mm(ps[3][:, :], ONE[0:1, 0:128], CS1[0:1, 0:512], True, True)
                cps(E[:, 1536:2048], ps[3][:, :])
                cps(D[:, 0:512], ps[1][:, :])
                vmul(D[:, 512:1024], D[:, 0:512], E[:, 1536:2048])
                dma(ao_scr[hd, :, qg * 512:(qg + 1) * 512], D[:, 512:1024])

        for qt in range(16):
            for hd in range(16):
                dma(C[:, hd * 128:(hd + 1) * 128], ao_scr[hd, :, qt * 128:(qt + 1) * 128])
            for hd in range(16):
                dma(D[:, 0:2048], woT[hd * 128:(hd + 1) * 128, :])
                for n4 in range(4):
                    mm(ps[n4][:, :], C[:, hd * 128:(hd + 1) * 128],
                       D[:, n4 * 512:(n4 + 1) * 512], hd == 0, hd == 15)
            dma(E[:, 0:2048], xo[qt * 128:(qt + 1) * 128, :])
            for n4 in range(4):
                vadd(E[:, 2048 + n4 * 512: 2048 + (n4 + 1) * 512],
                     ps[n4][:, :], E[:, n4 * 512:(n4 + 1) * 512])
            dma(x1_scr[qt * 128:(qt + 1) * 128, :], E[:, 2048:4096])
            layernorm(E[:, 2048:4096], D[:, 0:2048], E[:, 0:2048], D[:, 2048:4096])
            for hc in range(16):
                tr(ps[7][:, 0:128], D[:, 2048 + hc * 128: 2048 + (hc + 1) * 128])
                cps(CS1[:, 0:128], ps[7][:, 0:128])
                dma(l2_scr[hc, :, qt * 128:(qt + 1) * 128], CS1[:, 0:128])

        for ic in range(64):
            for j in range(4):
                dma(A[:, j * 2048:(j + 1) * 2048].rearrange("p (hc c) -> p hc c", c=128),
                    pg[j][:, ic * 128:(ic + 1) * 128].rearrange("(hc p) c -> p hc c", p=128))
                dma(A[:, 8192 + j * 2048: 8192 + (j + 1) * 2048].rearrange("p (hc c) -> p hc c", c=128),
                    cg[j][:, ic * 128:(ic + 1) * 128].rearrange("(hc p) c -> p hc c", p=128))
                dma(B[:, j * 2048:(j + 1) * 2048].rearrange("p (hc c) -> p hc c", c=128),
                    pu[j][:, ic * 128:(ic + 1) * 128].rearrange("(hc p) c -> p hc c", p=128))
            vmul(D[:, 0:2048], A[:, 0:2048], A[:, 8192:10240])
            for j in range(1, 4):
                vmul(E[:, 0:2048], A[:, j * 2048:(j + 1) * 2048],
                     A[:, 8192 + j * 2048: 8192 + (j + 1) * 2048])
                vadd(D[:, 0:2048], D[:, 0:2048], E[:, 0:2048])
            vmul(D[:, 2048:4096], B[:, 0:2048], A[:, 8192:10240])
            for j in range(1, 4):
                vmul(E[:, 0:2048], B[:, j * 2048:(j + 1) * 2048],
                     A[:, 8192 + j * 2048: 8192 + (j + 1) * 2048])
                vadd(D[:, 2048:4096], D[:, 2048:4096], E[:, 0:2048])
            dma(gw_scr[ic, :, :], D[:, 0:2048])
            dma(uw_scr[ic, :, :], D[:, 2048:4096])

        for tt in range(4):
            for hc in range(16):
                dma(B[:, hc * 512:(hc + 1) * 512],
                    l2_scr[hc, :, tt * 512:(tt + 1) * 512])
            for ic in range(64):
                dma(D[:, 0:2048], gw_scr[ic, :, :])
                dma(D[:, 2048:4096], uw_scr[ic, :, :])
                for hc in range(16):
                    mm(ps[0][:, :], D[:, hc * 128:(hc + 1) * 128],
                       B[:, hc * 512:(hc + 1) * 512], hc == 0, hc == 15)
                    mm(ps[1][:, :], D[:, 2048 + hc * 128: 2048 + (hc + 1) * 128],
                       B[:, hc * 512:(hc + 1) * 512], hc == 0, hc == 15)
                act(E[:, 0:512], ps[0][:, :], AF.Silu)
                vmul(E[:, 512:1024], E[:, 0:512], ps[1][:, :])
                dma(ia_scr[ic * 128:(ic + 1) * 128, tt * 512:(tt + 1) * 512],
                    E[:, 512:1024])

        for hc in range(4):
            for ic in range(64):
                half = A if ic < 32 else B
                w = half[:, (ic % 32) * 512:((ic % 32) + 1) * 512]
                for j in range(4):
                    dma(C[:, j * 512:(j + 1) * 512],
                        pd[j][ic * 128:(ic + 1) * 128, hc * 512:(hc + 1) * 512])
                    dma(E[:, j * 512:(j + 1) * 512],
                        cd[j][ic * 128:(ic + 1) * 128, hc * 512:(hc + 1) * 512])
                vmul(w, C[:, 0:512], E[:, 0:512])
                for j in range(1, 4):
                    vmul(D[:, 0:512], C[:, j * 512:(j + 1) * 512],
                         E[:, j * 512:(j + 1) * 512])
                    vadd(w, w, D[:, 0:512])
            for th in range(2):
                for ic in range(64):
                    half = A if ic < 32 else B
                    w = half[:, (ic % 32) * 512:((ic % 32) + 1) * 512]
                    dma(C[:, 0:1024], ia_scr[ic * 128:(ic + 1) * 128,
                                             th * 1024:(th + 1) * 1024])
                    for tg in range(8):
                        mm(ps[tg][:, :], C[:, tg * 128:(tg + 1) * 128], w,
                           ic == 0, ic == 63)
                for tg in range(8):
                    r0 = th * 1024 + tg * 128
                    dma(D[:, 0:512], x1_scr[r0:r0 + 128, hc * 512:(hc + 1) * 512])
                    vadd(D[:, 512:1024], ps[tg][:, :], D[:, 0:512])
                    dma(y[r0:r0 + 128, hc * 512:(hc + 1) * 512], D[:, 512:1024])

        TOTAL = cnt[0]

        with nc.Block() as blk:
            def runner(name):
                def _(eng):
                    for (g, pre, fn, inc) in sched:
                        if g == name:
                            eng.wait_ge(sem, pre)
                            fn(eng).then_inc(sem, inc)
                    eng.wait_ge(sem, TOTAL)
                return _
            blk.sync(runner("sync"))
            blk.tensor(runner("tensor"))
            blk.vector(runner("vector"))
            blk.scalar(runner("scalar"))
    return nc


def _grid():
    if "grid" in _CACHE:
        return _CACHE["grid"]
    t = np.linspace(0.0, NCTRL - 1.0, NS, dtype=np.float64)
    i = np.clip(np.floor(t).astype(np.int64), 0, NCTRL - 2)
    f = (t - i).astype(np.float32)
    del t
    idx = [np.clip(i + k, 0, NCTRL - 1).astype(np.int64) for k in (-1, 0, 1, 2)]
    f8 = f.astype(np.float64)
    f2 = f8 * f8
    f3 = f2 * f8
    cs = [0.5 * (-f8 + 2 * f2 - f3), 0.5 * (2.0 - 5 * f2 + 3 * f3),
          0.5 * (f8 + 4 * f2 - 3 * f3), 0.5 * (-f2 + f3)]
    cs = [c.astype(np.float32) for c in cs]
    cgu = [np.ascontiguousarray(c.reshape(8192, 2048).T) for c in cs]
    cdn = [np.ascontiguousarray(c.reshape(2048, 8192).T) for c in cs]
    _CACHE["grid"] = (idx, cgu, cdn)
    return _CACHE["grid"]


def _rope_tables():
    if "rope" in _CACHE:
        return _CACHE["rope"]
    inv = 1.0 / (10000.0 ** (np.arange(0, 128, 2, dtype=np.float64) / 128.0))
    fr = np.arange(4096, dtype=np.float64)[:, None] * inv[None, :]
    cosf = np.ascontiguousarray(np.repeat(np.cos(fr), 2, axis=1).astype(np.float32).T)
    sf = np.repeat(np.sin(fr), 2, axis=1).astype(np.float32)
    sT = sf.T.copy()
    sT[0:64, :] *= -1.0
    _CACHE["rope"] = (cosf, np.ascontiguousarray(sT))
    return _CACHE["rope"]


def kernel(L0_attn_x, L0_attn_Wq, L0_attn_Wk, L0_attn_Wv, L0_attn_Wo,
           L0_ffn_gate_cp, L0_ffn_up_cp, L0_ffn_down_cp):
    x = np.asarray(L0_attn_x, dtype=np.float32)
    wqT = np.ascontiguousarray(np.asarray(L0_attn_Wq, dtype=np.float32).T)
    wkT = np.ascontiguousarray(np.asarray(L0_attn_Wk, dtype=np.float32).T)
    wvT = np.ascontiguousarray(np.asarray(L0_attn_Wv, dtype=np.float32).T)
    woT = np.ascontiguousarray(np.asarray(L0_attn_Wo, dtype=np.float32).T)
    gcp = np.asarray(L0_ffn_gate_cp, dtype=np.float32)
    ucp = np.asarray(L0_ffn_up_cp, dtype=np.float32)
    dcp = np.asarray(L0_ffn_down_cp, dtype=np.float32)

    idx, cgu, cdn = _grid()
    pgs = [np.ascontiguousarray(gcp[ix].reshape(8192, 2048).T) for ix in idx]
    pus = [np.ascontiguousarray(ucp[ix].reshape(8192, 2048).T) for ix in idx]
    pds = [np.ascontiguousarray(dcp[ix].reshape(2048, 8192).T) for ix in idx]

    cosf, sinf = _rope_tables()
    ident = np.eye(128, dtype=np.float32)
    masks, cosos, sinos = [], [], []
    for hf in range(2):
        kk = np.arange(4096)[:, None]
        qq = hf * 2048 + np.arange(2048)[None, :]
        masks.append(np.where(qq >= kk - 128, 0.0, -1e30).astype(np.float32))
        cosos.append(np.ascontiguousarray(cosf[:, hf * 2048:(hf + 1) * 2048]))
        sinos.append(np.ascontiguousarray(sinf[:, hf * 2048:(hf + 1) * 2048]))

    if "nc" not in _CACHE:
        _CACHE["nc"] = build_nc()
    nc = _CACHE["nc"]

    in_maps = []
    for c in range(8):
        b, hf = c // 2, c % 2
        xfull = np.ascontiguousarray(x[b])
        m = {"xf": xfull,
             "xo": np.ascontiguousarray(xfull[hf * 2048:(hf + 1) * 2048]),
             "wqT": wqT, "wkT": wkT, "wvT": wvT, "woT": woT,
             "cosf": cosf, "sinf": sinf,
             "coso": cosos[hf], "sino": sinos[hf],
             "maskT": masks[hf], "identp": ident}
        for j in range(4):
            m[f"pg{j}"] = pgs[j]
            m[f"pu{j}"] = pus[j]
            m[f"cg{j}"] = cgu[j]
            m[f"pd{j}"] = pds[j]
            m[f"cd{j}"] = cdn[j]
        in_maps.append(m)

    res = run_bass_kernel_spmd(nc, in_maps, core_ids=list(range(8)))
    out = np.empty((4, 4096, 2048), dtype=np.float32)
    for c in range(8):
        b, hf = c // 2, c % 2
        out[b, hf * 2048:(hf + 1) * 2048, :] = res.results[c]["y"]
    return out
